# version 11
# baseline (speedup 1.0000x reference)
import numpy as np
import ml_dtypes
import concourse.bass as bass
import concourse.mybir as mybir
from concourse.bass_utils import run_bass_kernel_spmd

F32 = mybir.dt.float32
BF16 = mybir.dt.bfloat16
I32 = mybir.dt.int32
AF = mybir.ActivationFunctionType
ALU = mybir.AluOpType

D = 4096
FT = 32
NT = 32
TOKC = 2048
T = 512
NCH = TOKC // T
SL = 512
NTC = SL // 128
WE = 128 * D
CSO = NTC * WE
P1B = NTC * WE + 3 * SL
P2B = 2 * NTC * WE
WFO = P1B
WBO = P1B + NTC * WE
PAYB = P1B + P2B
EPS = 1e-5

PASSES = [
    ("attn_a", 0, True, False, "b2cs"),
    ("ffn", 1, False, True, "bfcs"),
    ("attn_b", 2, False, False, "bbcs"),
    ("ffn", 1, False, True, "bfcs"),
]


def sap(h, offset, pattern):
    base = h[tuple(slice(0, s) for s in h.shape)]
    return bass.AP(tensor=base.tensor, offset=offset, ap=[list(p) for p in pattern])


def full(h):
    return h[tuple(slice(0, s) for s in h.shape)]


class Ctx:
    def __init__(self, nc, me, eng, H, P, sems):
        self.nc, self.me, self.eng, self.H, self.P, self.sems = nc, me, eng, H, P, sems
        self.T = eng if me == "tensor" else None
        self.V = eng if me == "vector" else None
        self.C = eng if me == "scalar" else None
        self.S = eng if me == "sync" else None
        self.G = eng if me == "gpsimd" else None
        self.c = {k: 0 for k in sems}

    def wait(self, key, n):
        if n and n > 0:
            self.eng.wait_ge(self.sems[key], n * (16 if key[0] == "d" else 1))

    def vw(self):
        if self.V:
            self.V.wait_ge(self.sems["v"], self.c["v"])

    def inc(self, ins, key):
        self.c[key] += 1
        if ins is not None:
            ins.then_inc(self.sems[key], 16 if key[0] == "d" else 1)
        return self.c[key]


def build_nc():
    nc = bass.Bass("TRN2", target_bir_lowering=False)
    H = {}

    def din(name, shape, dt):
        H[name] = nc.dram_tensor(name, shape, dt, kind="ExternalInput")

    din("xT", [D, TOKC], F32)
    din("w1i", [FT * 128, FT * 128], F32)
    din("w2t", [D, SL], F32)
    din("maskS", [D, SL], F32)
    din("wfi", [NTC * 128, FT * 128], F32)
    din("wbt", [D, SL], F32)
    din("maskB", [D, SL], I32)
    din("b2c", [128, NT], F32)
    din("bfc", [128, NT], F32)
    din("bbc", [128, NT], F32)
    din("ug", [128, FT * 68], F32)
    din("bt", [64, D], F32)
    din("gbr", [1, 4], F32)
    din("iotac", [64, 1], F32)
    din("b32", [128, 4], BF16)
    din("b32t", [4, 128], BF16)
    H["outT"] = nc.dram_tensor("outT", [D, TOKC], F32, kind="ExternalOutput")
    H["pay"] = nc.dram_tensor("pay", [1, PAYB], BF16)
    H["g1"] = nc.dram_tensor("g1", [8, P1B], BF16, addr_space="Shared")
    H["g2"] = nc.dram_tensor("g2", [8, P2B], BF16, addr_space="Shared")
    H["rA"] = nc.dram_tensor("rA", [D, T], F32)
    H["rB"] = nc.dram_tensor("rB", [D, T], F32)
    H["statB"] = nc.dram_tensor("statB", [1, (NCH - 1) * 2 * T], F32)

    from contextlib import ExitStack
    with ExitStack() as ctx:
        en = ctx.enter_context

        def sb(name, shape, dt):
            H[name] = en(nc.sbuf_tensor(name, shape, dt))

        sb("rb0", [128, FT * T], BF16)
        sb("rb1", [128, FT * T], BF16)
        sb("wt0", [128, D], BF16)
        sb("wt1", [128, D], BF16)
        sb("ab", [128, T], F32)
        sb("mb", [128, T], F32)
        sb("e2b", [128, T], F32)
        sb("sdb", [128, T], F32)
        sb("nmb", [1, T], BF16)
        sb("t1a", [128, T], F32)
        sb("t1b", [128, T], F32)
        sb("t2a", [128, T], F32)
        sb("t2b", [128, T], F32)
        sb("roa", [128, T], F32)
        sb("rob", [128, T], F32)
        sb("sqa", [128, T], BF16)
        sb("sqb_", [128, T], BF16)
        sb("csb", [1, D], BF16)
        sb("ugb", [128, FT * 68], BF16)
        sb("btb", [64, D], BF16)
        sb("iob", [64, T], BF16)
        sb("onesb", [128, 128], BF16)
        sb("csug", [1, 128], BF16)
        sb("gbrow", [1, 4], F32)
        sb("iocs", [64, 1], F32)
        sb("b32s", [128, 4], BF16)
        sb("b32ts", [4, 128], BF16)
        sb("b2cs", [128, NT], F32)
        sb("bfcs", [128, NT], F32)
        sb("bbcs", [128, NT], F32)
        sb("epsc", [128, 1], F32)
        sb("sc4k", [128, 1], F32)
        sb("um", [64, T], BF16)
        sb("srows", [1, 4 * T], F32)
        sb("thrb", [1, T], BF16)
        sb("rows", [1, 8 * T], F32)
        sb("mi32a", [128, SL], I32)
        sb("mi32b", [128, SL], I32)
        sb("mfa", [128, 1024], F32)
        sb("smka", [128, SL], BF16)
        sb("smkb", [128, SL], BF16)
        sb("bs16", [128, 16], BF16)
        sb("gind", [4, SL], BF16)
        sb("gexp", [128, SL], F32)
        sb("csrow", [1, SL], BF16)
        sb("xca", [128, T], BF16)
        sb("xcb", [128, T], BF16)

        P = {}
        for nm in ("o0", "o1", "sx", "sq", "a4", "a5", "a6", "a7"):
            P[nm] = en(nc.psum_tensor("ps_" + nm, [128, T], F32))

        sems = {k: en(nc.semaphore("sem_" + k))
                for k in ("p", "v", "c", "g", "dg", "ds", "dc", "cc")}

        blk = en(nc.Block())
        for me, attach in (("tensor", blk.tensor), ("vector", blk.vector),
                           ("scalar", blk.scalar), ("sync", blk.sync),
                           ("gpsimd", blk.gpsimd)):
            attach(lambda eng, me=me: program(Ctx(nc, me, eng, H, P, sems)))
    return nc


def program(E):
    st = {}
    phase0(E, st)
    for ci in range(NCH):
        for si, spec in enumerate(PASSES):
            emit_pass(E, st, ci, si, spec)
    if E.C:
        E.wait("dc", E.c["dc"])


def phase0(E, st):
    H, P = E.H, E.P

    E.inc(E.G.memset(H["onesb"][:, :], 1.0) if E.G else None, "g")
    E.inc(E.G.memset(H["epsc"][:, :], EPS) if E.G else None, "g")
    E.inc(E.G.memset(H["sc4k"][:, :], 1.0 / D) if E.G else None, "g")
    E.inc(E.G.memset(H["ab"][0:64, :], 1.0) if E.G else None, "g")
    init_g = E.c["g"]
    st["init_g"] = init_g

    for dstn, srcn in (("gbrow", "gbr"), ("iocs", "iotac"), ("b32s", "b32"),
                       ("b32ts", "b32t"), ("b2cs", "b2c"), ("bfcs", "bfc"),
                       ("bbcs", "bbc")):
        E.inc(E.S.dma_start(out=full(H[dstn]), in_=full(H[srcn]))
              if E.S else None, "ds")
    n_smalls = E.c["ds"]

    if E.V:
        E.wait("g", init_g)
        E.wait("ds", n_smalls)
    E.inc(E.V.tensor_scalar(H["iob"][:, :], H["ab"][0:64, 0:T], H["iocs"][:, 0:1],
                            None, ALU.mult) if E.V else None, "v")

    mfa = H["mfa"]
    folds = [(0, 512), (512, 256), (768, 128), (896, 64), (960, 32)]
    mi = [H["mi32a"], H["mi32b"]]
    smk = [H["smka"], H["smkb"]]
    if E.T:
        E.wait("ds", n_smalls)
        E.wait("g", init_g)
    conv_v = {}
    gp_tile = {}
    for kj in range(NT):
        if E.S and kj >= 2:
            E.wait("v", conv_v[kj - 2])
        src = sap(H["maskB"], kj * 128 * SL, [[SL, 128], [1, SL]])
        n_mi = E.inc(E.S.dma_start(out=mi[kj % 2][:, :], in_=src)
                     if E.S else None, "ds")
        if E.V:
            E.wait("ds", n_mi)
        E.vw()
        E.inc(E.V.tensor_copy(mfa[:, 0:512], mi[kj % 2][:, :]) if E.V else None, "v")
        conv_v[kj] = E.c["v"]
        for off, n in folds:
            ev = sap(mfa, off, [[1024, 128], [2, n // 2]])
            od = sap(mfa, off + 1, [[1024, 128], [2, n // 2]])
            E.vw()
            if n > 32:
                E.inc(E.V.tensor_tensor(mfa[:, off + n:off + n + n // 2], ev, od,
                                        ALU.add) if E.V else None, "v")
            else:
                E.inc(E.V.tensor_tensor(mfa[:, 992:1008], ev, od, ALU.add)
                      if E.V else None, "v")
                if E.V and kj >= 1:
                    E.wait("p", gp_tile[kj - 1])
                E.vw()
                E.inc(E.V.tensor_copy(H["bs16"][:, :], mfa[:, 992:1008])
                      if E.V else None, "v")
        E.vw()
        E.inc(E.V.tensor_copy(H["gexp"][:, 0:16], H["bs16"][:, :])
              if E.V else None, "v")
        vf = E.c["v"]
        if E.T:
            E.wait("v", vf)
        ins = (E.T.matmul(P["a6"][0:4, kj * 16:(kj + 1) * 16], H["b32s"][:, :],
                          H["bs16"][:, :], start=(kj == 0), stop=(kj == NT - 1))
               if E.T else None)
        gp_tile[kj] = E.inc(ins, "p")
    g_p = gp_tile[NT - 1]
    if E.V:
        E.wait("p", g_p)
    E.inc(E.V.tensor_scalar(H["gind"][:, :], P["a6"][0:4, 0:SL], 0.0, None,
                            ALU.is_gt) if E.V else None, "v")
    E.vw()
    E.inc(E.V.tensor_copy(mfa[0:4, 0:16], H["gind"][0:4, 0:16])
          if E.V else None, "v")
    gind_v = E.c["v"]
    if E.T:
        E.wait("v", gind_v)
    for kj in range(NT):
        ins = (E.T.matmul(P["a7"][:, kj * 16:(kj + 1) * 16], H["b32ts"][:, :],
                          H["gind"][:, kj * 16:(kj + 1) * 16],
                          start=(kj == 0), stop=(kj == NT - 1)) if E.T else None)
    ge_p = E.inc(ins, "p")
    if E.V:
        E.wait("p", ge_p)
    E.inc(E.V.tensor_copy(H["gexp"][:, :], P["a7"][:, 0:SL]) if E.V else None, "v")
    gexp_v = E.c["v"]

    if E.G:
        E.wait("v", gexp_v)
    n_ugb = E.inc(E.G.dma_start(out=H["ugb"][:, :], in_=full(H["ug"]))
                  if E.G else None, "dg")
    E.inc(E.G.dma_start(out=H["btb"][:, :], in_=full(H["bt"]))
          if E.G else None, "dg")
    src = sap(H["w2t"], 0, [[SL, 128], [SL * 128, FT], [1, SL]])
    n_w2 = E.inc(E.G.dma_start(out=H["rb1"][:, :], in_=src) if E.G else None, "dg")
    src = sap(H["xT"], 0, [[TOKC, 128], [TOKC * 128, FT], [1, T]])
    n_rb0 = E.inc(E.G.dma_start(out=H["rb0"][:, :], in_=src) if E.G else None, "dg")
    st["rb0_dg"] = n_rb0
    dst = sap(H["pay"], WFO, [[1, NTC * WE]])
    n_wfi = E.inc(E.G.dma_start(out=dst, in_=full(H["wfi"])) if E.G else None, "dg")

    wbb = [H["t2a"], H["t2b"]]
    wbpay = {}
    wb_smk_v = {}
    csB_tile = {}
    for kj in range(NT):
        if E.S and kj >= 2:
            E.wait("v", wb_smk_v[kj - 2])
        src = sap(H["wbt"], kj * 128 * SL, [[SL, 128], [1, SL]])
        n_wb = E.inc(E.S.dma_start(out=wbb[kj % 2][:, :], in_=src)
                     if E.S else None, "ds")
        if E.V:
            E.wait("ds", n_wb)
            E.wait("v", gexp_v)
            if kj >= 2:
                E.wait("dc", wbpay[kj - 2])
                E.wait("p", csB_tile[kj - 2])
        for nb in range(16):
            E.inc(E.V.tensor_scalar(smk[kj % 2][:, nb * 32:(nb + 1) * 32],
                                    wbb[kj % 2][:, nb * 32:(nb + 1) * 32],
                                    H["gexp"][:, kj * 16 + nb:kj * 16 + nb + 1],
                                    None, ALU.mult) if E.V else None, "v")
        smk_v = E.c["v"]
        wb_smk_v[kj] = smk_v
        if E.T:
            E.wait("v", smk_v)
        ins = (E.T.matmul(P["a5"][0:1, 0:SL], H["onesb"][:, 0:1], smk[kj % 2][:, :],
                          start=(kj == 0), stop=(kj == NT - 1)) if E.T else None)
        csB_tile[kj] = E.inc(ins, "p")
        if E.C:
            E.wait("v", smk_v)
        dst = sap(H["pay"], WBO + kj * 128, [[D, 128], [WE, NTC], [1, 128]])
        wbpay[kj] = E.inc(E.C.dma_start(out=dst, in_=smk[kj % 2][:, :])
                          if E.C else None, "dc")
    csb_p = E.c["p"]
    if E.V:
        E.wait("p", csb_p)
    E.vw()
    E.inc(E.V.tensor_copy(H["csrow"][0:1, :], P["a5"][0:1, 0:SL])
          if E.V else None, "v")
    csB_v = E.c["v"]
    if E.S:
        E.wait("v", csB_v)
    n_csB_s = E.inc(E.S.dma_start(out=sap(H["pay"], CSO + 2 * SL, [[1, SL]]),
                                  in_=H["csrow"][0:1, :]) if E.S else None, "ds")

    wts = [H["wt0"], H["wt1"]]
    msb = [H["t1a"], H["t1b"]]
    spar = [P["a6"], P["a7"]]
    xcs = [H["xca"], H["xcb"]]
    sqs = [H["sqa"], H["sqb_"]]
    s_ev_v = {}
    sbuild_p = {}
    spay = {}
    csS_tile = {}
    st["sq_guard"] = {0: 0, 1: 0}
    xq_dg = {}
    xq_list = [(1, j) for j in range(FT)] + [(2, j) for j in range(FT)] \
        + [(3, j) for j in range(FT)]
    xq_emit = 0
    xc_guard = {}
    stat_marks = {}

    def emit_xload(cb, j):
        par = j % 2
        if E.G:
            gm = xc_guard.get((cb, j), 0)
            E.wait("p", gm)
        src_ = sap(E.H["xT"], cb * T + j * 128 * TOKC, [[TOKC, 128], [1, T]])
        xq_dg[(cb, j)] = E.inc(E.G.dma_start(out=xcs[par][:, :], in_=src_)
                               if E.G else None, "dg")

    def chunk_stats(key, j, xtile, pbank_x, pbank_q):
        if E.C:
            if key == 0:
                E.wait("dg", n_rb0)
            else:
                E.wait("dg", xq_dg[(key, j)])
            E.wait("p", st["sq_guard"][j % 2])
        E.inc(E.C.activation(sqs[j % 2][:, :], xtile, AF.Square)
              if E.C else None, "c")
        sq_c = E.c["c"]
        if E.T:
            E.wait("c", sq_c)
            if j == 0:
                E.wait("v", stat_marks.get(("bankfree", key), 0))
        ins = (E.T.matmul(pbank_x[:, :], H["onesb"][:, :], xtile,
                          start=(j == 0), stop=(j == FT - 1)) if E.T else None)
        ins = (E.T.matmul(pbank_q[:, :], H["onesb"][:, :], sqs[j % 2][:, :],
                          start=(j == 0), stop=(j == FT - 1)) if E.T else None)
        n = E.inc(ins, "p")
        st["sq_guard"][j % 2] = n
        stat_marks[(key, j)] = n
        if key in (1, 2, 3):
            nxt = xq_list.index((key, j)) + 2
            if nxt < len(xq_list):
                xc_guard[xq_list[nxt]] = n

    for stl in range(NT):
        par = stl % 2
        if E.G and stl >= 2:
            E.wait("p", sbuild_p[stl - 2])
        src_ = sap(H["w1i"], stl * 128 * D, [[D, 128], [1, D]])
        n_w1 = E.inc(E.G.dma_start(out=wts[par][:, :], in_=src_)
                     if E.G else None, "dg")
        if xq_emit < FT:
            emit_xload(*xq_list[xq_emit])
            xq_emit += 1
        if E.S and stl >= 2:
            E.wait("v", s_ev_v[stl - 2])
        src_ = sap(H["maskS"], stl * 128 * SL, [[SL, 128], [1, SL]])
        n_ms = E.inc(E.S.dma_start(out=msb[par][:, :], in_=src_)
                     if E.S else None, "ds")
        if E.T:
            E.wait("dg", max(n_w1, n_w2))
            if stl < 2:
                E.wait("v", gind_v if par == 0 else gexp_v)
            else:
                E.wait("v", s_ev_v[stl - 2])
        for dblk in range(FT):
            ins = (E.T.matmul(spar[par][:, 0:SL],
                              wts[par][:, dblk * 128:(dblk + 1) * 128],
                              H["rb1"][:, dblk * T:(dblk + 1) * T],
                              start=(dblk == 0), stop=(dblk == FT - 1))
                   if E.T else None)
        sbuild_p[stl] = E.inc(ins, "p")
        if E.V:
            E.wait("p", sbuild_p[stl])
            E.wait("ds", n_ms)
            if stl < 2:
                E.wait("dc", wbpay[NT - 2 + stl])
                E.wait("p", csB_tile[NT - 2 + stl])
            else:
                E.wait("dc", spay[stl - 2])
                E.wait("p", csS_tile[stl - 2])
        E.vw()
        E.inc(E.V.tensor_tensor(smk[par][:, :], spar[par][:, 0:SL],
                                msb[par][:, :], ALU.mult) if E.V else None, "v")
        s_ev_v[stl] = E.c["v"]
        if E.T:
            E.wait("v", s_ev_v[stl])
        ins = (E.T.matmul(P["a4"][0:1, 0:SL], H["onesb"][:, 0:1], smk[par][:, :],
                          start=(stl == 0), stop=(stl == NT - 1)) if E.T else None)
        csS_tile[stl] = E.inc(ins, "p")
        if E.C:
            E.wait("v", s_ev_v[stl])
        dst = sap(H["pay"], stl * 128, [[D, 128], [WE, NTC], [1, 128]])
        spay[stl] = E.inc(E.C.dma_start(out=dst, in_=smk[par][:, :])
                          if E.C else None, "dc")
        chunk_stats(0, stl, H["rb0"][:, stl * T:(stl + 1) * T], P["sx"], P["sq"])
        chunk_stats(1, stl, xcs[stl % 2][:, :], P["o0"], P["o1"])
    st["stats_p"] = stat_marks[(0, NT - 1)]
    csS_p = E.c["p"]
    if E.V:
        E.wait("p", csS_p)
    E.vw()
    E.inc(E.V.tensor_copy(H["csrow"][0:1, :], P["a4"][0:1, 0:SL])
          if E.V else None, "v")
    csS_v = E.c["v"]
    if E.S:
        E.wait("v", csS_v)
        E.wait("ds", n_csB_s)
    n_csS_s = E.inc(E.S.dma_start(out=sap(H["pay"], CSO, [[1, SL]]),
                                  in_=H["csrow"][0:1, :]) if E.S else None, "ds")


    def rows_evac(key, pbank_x, pbank_q):
        if E.V:
            E.wait("p", stat_marks[(key, NT - 1)])
            E.wait("ds", st["statB_ds"])
        E.vw()
        E.inc(E.V.tensor_scalar(H["rows"][0:1, 0:T], pbank_x[0:1, 0:T],
                                1.0 / D, None, ALU.mult) if E.V else None, "v")
        E.inc(E.V.tensor_scalar(H["rows"][0:1, T:2 * T], pbank_q[0:1, 0:T],
                                1.0 / D, None, ALU.mult) if E.V else None, "v")
        mk = E.c["v"]
        if E.S:
            E.wait("v", mk)
        st["statB_ds"] = E.inc(
            E.S.dma_start(out=sap(H["statB"], (key - 1) * 2 * T, [[1, 2 * T]]),
                          in_=H["rows"][0:1, 0:2 * T]) if E.S else None, "ds")
        return mk

    st["statB_ds"] = 0
    r1_v = rows_evac(1, P["o0"], P["o1"])
    stat_marks[("bankfree", 3)] = r1_v

    csw_tile_p = 0
    for t in range(NTC):
        if E.S:
            E.wait("dg", n_wfi)
            if t == 0:
                E.wait("p", sbuild_p[NT - 1])
            else:
                E.wait("p", csw_tile_p)
        src_ = sap(H["pay"], WFO + t * WE, [[D, 128], [1, D]])
        n_wf = E.inc(E.S.dma_start(out=H["wt0"][:, :], in_=src_)
                     if E.S else None, "ds")
        if E.T:
            E.wait("ds", n_wf)
            if t == 0:
                E.wait("v", s_ev_v[NT - 2])
        for j in range(FT):
            ins = (E.T.matmul(P["a6"][0:1, t * 128:(t + 1) * 128],
                              H["onesb"][:, 0:1], H["wt0"][:, j * 128:(j + 1) * 128],
                              start=(t == 0 and j == 0),
                              stop=(t == NTC - 1 and j == FT - 1)) if E.T else None)
        csw_tile_p = E.inc(ins, "p")
    if E.V:
        E.wait("p", csw_tile_p)
        E.wait("ds", n_csS_s)
    E.vw()
    E.inc(E.V.tensor_copy(H["csrow"][0:1, :], P["a6"][0:1, 0:SL])
          if E.V else None, "v")
    csW_v = E.c["v"]
    if E.S:
        E.wait("v", csW_v)
    n_csW_s = E.inc(E.S.dma_start(out=sap(H["pay"], CSO + 1 * SL, [[1, SL]]),
                                  in_=H["csrow"][0:1, :]) if E.S else None, "ds")

    if E.T:
        E.wait("dg", n_ugb)
        E.wait("v", s_ev_v[NT - 1])
    for j in range(FT):
        ins = (E.T.matmul(P["a7"][0:1, 0:68], H["onesb"][:, 0:1],
                          H["ugb"][:, j * 68:(j + 1) * 68],
                          start=(j == 0), stop=(j == FT - 1)) if E.T else None)
    csug_p = E.inc(ins, "p")
    if E.V:
        E.wait("p", csug_p)
    E.vw()
    E.inc(E.V.tensor_copy(H["csug"][0:1, 0:68], P["a7"][0:1, 0:68])
          if E.V else None, "v")
    csug_v = E.c["v"]
    stat_marks[("bankfree", 2)] = max(csW_v, csug_v)

    if E.G:
        E.wait("ds", n_csW_s)
        E.wait("dc", spay[NT - 1])
        ins = E.G.collective_compute(
            "AllGather", ALU.bypass, replica_groups=[list(range(8))],
            ins=[sap(H["pay"], 0, [[1, P1B]])],
            outs=[H["g1"].ap().opt()])
    else:
        ins = None
    st["cc1_n"] = E.inc(ins, "cc")


    for j in range(FT):
        emit_xload(2, j)
        chunk_stats(2, j, xcs[j % 2][:, :], P["a6"], P["a7"])
    r2_v = rows_evac(2, P["a6"], P["a7"])
    for j in range(FT):
        emit_xload(3, j)
        chunk_stats(3, j, xcs[j % 2][:, :], P["o0"], P["o1"])
    r3_v = rows_evac(3, P["o0"], P["o1"])

    if E.G:
        E.wait("ds", n_csW_s)
        E.wait("dc", spay[NT - 1])
        E.wait("dg", n_wfi)
        ins = E.G.collective_compute(
            "AllGather", ALU.bypass, replica_groups=[list(range(8))],
            ins=[sap(H["pay"], P1B, [[1, P2B]])],
            outs=[H["g2"].ap().opt()])
    else:
        ins = None
    st["cc2_n"] = E.inc(ins, "cc")

    st["bnd_v_done"] = 0
    st["corr_p_done"] = 0
    st["um_v"] = csS_v
    st["mask_v"] = csB_v
    st["ev"] = {NT - 2: r3_v, NT - 1: r3_v}
    st["wf"] = {NT - 2: csw_tile_p, NT - 1: csw_tile_p}
    st["ro_free_v"] = {}
    st["t2_free_v"] = {}
    st["rbcp_v"] = {}
    st["sq_v"] = {}
    st["bnd_c_psum_read"] = 0
    st["mm_done_prev_pass"] = 0
    st["recast_dg"] = 0


def boundary(E, st, ci, si):
    H, P = E.H, E.P
    if si == 0 and ci > 0:
        if E.G:
            E.wait("v", st["bnd_v_done"])
        src_m = sap(H["statB"], (ci - 1) * 2 * T, [[0, 128], [1, T]])
        src_e = sap(H["statB"], (ci - 1) * 2 * T + T, [[0, 128], [1, T]])
        E.inc(E.G.dma_start(out=H["mb"][:, :], in_=src_m) if E.G else None, "dg")
        n2 = E.inc(E.G.dma_start(out=H["e2b"][:, :], in_=src_e)
                   if E.G else None, "dg")
        if E.V:
            E.wait("dg", n2)
    else:
        if E.C:
            E.wait("p", st["stats_p"])
            E.wait("v", st["bnd_v_done"])
            E.wait("g", st["init_g"])
        E.inc(E.C.activation(H["mb"][:, :], P["sx"][:, :], AF.Copy,
                             scale=H["sc4k"][:, 0:1]) if E.C else None, "c")
        E.inc(E.C.activation(H["e2b"][:, :], P["sq"][:, :], AF.Copy,
                             scale=H["sc4k"][:, 0:1]) if E.C else None, "c")
        st["bnd_c_psum_read"] = E.c["c"]
        if E.V:
            E.wait("c", E.c["c"])
    E.vw()
    E.inc(E.V.tensor_tensor(H["sdb"][:, :], H["mb"][:, :], H["mb"][:, :], ALU.mult)
          if E.V else None, "v")
    E.vw()
    E.inc(E.V.tensor_tensor(H["sdb"][:, :], H["e2b"][:, :], H["sdb"][:, :],
                            ALU.subtract) if E.V else None, "v")
    var_v = E.c["v"]
    if E.C:
        E.wait("v", var_v)
    E.inc(E.C.activation(H["e2b"][:, :], H["sdb"][:, :], AF.Sqrt,
                         bias=H["epsc"][:, 0:1]) if E.C else None, "c")
    sd_c = E.c["c"]
    if E.V:
        E.wait("p", st["corr_p_done"])
    E.vw()
    E.inc(E.V.tensor_scalar(H["nmb"][0:1, :], H["mb"][0:1, :], -1.0, None, ALU.mult)
          if E.V else None, "v")
    E.vw()
    E.inc(E.V.tensor_copy(H["rows"][0:1, 6 * T:7 * T], H["nmb"][0:1, :])
          if E.V else None, "v")
    st["negm_v"] = E.c["v"]
    if E.V:
        E.wait("c", sd_c)
    E.vw()
    E.inc(E.V.reciprocal(H["ab"][:, :], H["e2b"][:, :]) if E.V else None, "v")
    st["ab_v"] = E.c["v"]
    st["bnd_v_done"] = E.c["v"]


def ffn_prestep_u(E, st):
    H, P = E.H, E.P
    pu = P["a4"]
    if E.T:
        E.wait("v", st["negm_v"])
        E.wait("v", st["um_v"])
    for j in range(FT):
        ins = (E.T.matmul(pu[0:68, :], H["ugb"][:, j * 68:(j + 1) * 68],
                          H[st["rbcur"]][:, j * T:(j + 1) * T],
                          start=(j == 0), stop=False) if E.T else None)
    ins = (E.T.matmul(pu[0:68, :], H["csug"][0:1, 0:68], H["nmb"][0:1, :],
                      start=False, stop=True) if E.T else None)
    st["u_p"] = E.inc(ins, "p")
    if E.V:
        E.wait("p", st["u_p"])
    E.vw()
    E.inc(E.V.tensor_copy(H["t2a"][64:68, :], pu[64:68, :]) if E.V else None, "v")
    sc_v = E.c["v"]
    if E.G:
        E.wait("v", sc_v)
    nfd = E.inc(E.G.dma_start(out=H["srows"][0:1, :], in_=H["t2a"][64:68, :])
                if E.G else None, "dg")
    R = H["rows"]
    seg = lambda k: R[0:1, k * T:(k + 1) * T]
    if E.V:
        E.wait("dg", nfd)
        E.wait("ds", st.get("statB_ds", 0))
    for e in range(4):
        E.inc(E.V.scalar_tensor_tensor(seg(e), H["e2b"][0:1, 0:T],
                                       H["gbrow"][0:1, e:e + 1],
                                       H["srows"][0:1, e * T:(e + 1) * T],
                                       ALU.mult, ALU.add) if E.V else None, "v")
    E.vw()
    E.inc(E.V.tensor_tensor(seg(5), seg(1), seg(0), ALU.is_gt) if E.V else None, "v")
    E.vw()
    E.inc(E.V.tensor_scalar(seg(7), seg(5), 8.0, 8.0, ALU.mult, ALU.add)
          if E.V else None, "v")
    E.inc(E.V.tensor_tensor(seg(4), seg(0), seg(1), ALU.max) if E.V else None, "v")
    for e, rv in ((2, 32.0), (3, 64.0)):
        E.vw()
        E.inc(E.V.tensor_tensor(seg(5), seg(e), seg(4), ALU.is_gt)
              if E.V else None, "v")
        E.vw()
        E.inc(E.V.tensor_scalar(seg(6), seg(7), -1.0, rv, ALU.mult, ALU.add)
              if E.V else None, "v")
        E.vw()
        E.inc(E.V.tensor_tensor(seg(6), seg(6), seg(5), ALU.mult)
              if E.V else None, "v")
        E.vw()
        E.inc(E.V.tensor_tensor(seg(7), seg(7), seg(6), ALU.add)
              if E.V else None, "v")
        if e == 2:
            E.vw()
            E.inc(E.V.tensor_tensor(seg(4), seg(4), seg(2), ALU.max)
                  if E.V else None, "v")
    E.vw()
    E.inc(E.V.tensor_copy(H["thrb"][0:1, :], seg(7)) if E.V else None, "v")
    E.vw()
    E.inc(E.V.tensor_copy(seg(5), H["thrb"][0:1, :]) if E.V else None, "v")
    st["thr_v"] = E.c["v"]


def ffn_prestep_mask(E, st):
    H, P = E.H, E.P
    if E.T:
        E.wait("v", st["thr_v"])
        E.wait("v", st["mask_v"])
    ins = (E.T.matmul(P["a5"][0:64, :], H["onesb"][0:1, 0:64], H["thrb"][0:1, :],
                      start=True, stop=True) if E.T else None)
    thr_p = E.inc(ins, "p")
    if E.V:
        E.wait("p", thr_p)
    E.vw()
    E.inc(E.V.tensor_tensor(H["t1b"][0:64, :], H["iob"][:, :], P["a5"][0:64, :],
                            ALU.is_lt) if E.V else None, "v")
    st["mask_v"] = E.c["v"]
    E.vw()
    E.inc(E.V.tensor_tensor(H["um"][:, :], H["t1b"][0:64, :], P["a4"][0:64, :],
                            ALU.mult) if E.V else None, "v")
    E.vw()
    E.inc(E.V.tensor_copy(H["t1b"][0:64, 0:16], H["um"][0:64, 0:16])
          if E.V else None, "v")
    st["um_v"] = E.c["v"]


def emit_pass(E, st, ci, si, spec):
    name, wi, gelu, lora, biask = spec
    H, P = E.H, E.P
    last_pass = (si == len(PASSES) - 1)
    first = (ci == 0 and si == 0)
    rbcur = "rb0" if si % 2 == 0 else "rb1"
    rbnxt = "rb1" if si % 2 == 0 else "rb0"
    st["rbcur"] = rbcur
    rold_src = ("xT", ci * T) if si == 0 else (("rA", 0) if si in (1, 3) else ("rB", 0))
    rnew_dst = ("outT", ci * T) if last_pass else (("rA", 0) if si in (0, 2) else ("rB", 0))
    bias = H[biask]

    wf_prev = st["wf"]
    ev_prev = st["ev"]
    ro_free_prev = st["ro_free_v"]
    t2_free_prev = st["t2_free_v"]
    st["wf"] = {}
    st["ev"] = {}
    st["ro_free_v"] = {}
    st["t2_free_v"] = {}
    st["rbcp_v"] = {}
    st["sq_v"] = {}
    st["grp"] = {}
    gelu_c_prev = st.get("gelu_c", {})
    st["gelu_c"] = {}

    wslots = ["wt0", "wt1"]
    wl = {}

    gname = "g1" if wi == 0 else "g2"
    grow = P1B if wi == 0 else P2B
    gbase = 0 if wi == 0 else (wi - 1) * NTC * WE
    cc_need = st["cc1_n"] if wi == 0 else st["cc2_n"]

    def load_wt(nt):
        if E.S:
            if nt < 2:
                E.wait("p", wf_prev.get(NT - 2 + nt, 0))
                if first or (ci == 0 and si == 1):
                    E.wait("cc", cc_need)
            else:
                E.wait("p", st["wf"][nt - 2])
        r, sub = nt // 4, nt % 4
        src = sap(H[gname], r * grow + gbase + sub * WE, [[D, 128], [1, D]])
        wl[nt] = E.inc(E.S.dma_start(out=H[wslots[nt % 2]][:, :], in_=src)
                       if E.S else None, "ds")

    load_wt(0)
    load_wt(1)
    if E.S:
        E.wait("cc", st["cc1_n"])
        E.wait("p", st["corr_p_done"])
    src_cs = sap(H["g1"], CSO + wi * SL, [[P1B, 8], [1, SL]])
    cs_ds = E.inc(E.S.dma_start(out=H["csb"][0:1, :], in_=src_cs)
                  if E.S else None, "ds")

    boundary(E, st, ci, si)

    if si == 3 and ci + 1 < NCH:
        if E.G:
            E.wait("p", st["mm_done_prev_pass"])
        src = sap(H["xT"], (ci + 1) * T, [[TOKC, 128], [TOKC * 128, FT], [1, T]])
        st["recast_dg"] = E.inc(E.G.dma_start(out=H["rb0"][:, :], in_=src)
                                if E.G else None, "dg")

    if lora:
        ffn_prestep_u(E, st)

    def mains(nt):
        po = P["o0"] if nt % 2 == 0 else P["o1"]
        slot = wslots[nt % 2]
        if E.T:
            E.wait("ds", wl[nt])
            if nt >= 2:
                E.wait("v", st["ev"][nt - 2])
            else:
                E.wait("v", ev_prev.get(NT - 2 + nt, 0))
            if si == 0 and ci > 0 and nt == 0:
                E.wait("dg", st["recast_dg"])
            if first and nt == 0:
                E.wait("dg", st["rb0_dg"])
        for j in range(FT):
            ins = (E.T.matmul(po[:, :], H[slot][:, j * 128:(j + 1) * 128],
                              H[rbcur][:, j * T:(j + 1) * T],
                              start=(j == 0), stop=False) if E.T else None)
            if j == FT - 1:
                st["wf"][nt] = E.inc(ins, "p")
        if nt + 2 < NT:
            load_wt(nt + 2)

    def finish(nt):
        po = P["o0"] if nt % 2 == 0 else P["o1"]
        if E.T and nt == 0:
            E.wait("v", st["negm_v"])
            E.wait("ds", cs_ds)
        ins = (E.T.matmul(po[:, :], H["csb"][0:1, nt * 128:(nt + 1) * 128],
                          H["nmb"][0:1, :], start=False, stop=not lora)
               if E.T else None)
        if lora:
            if E.T and nt == 0:
                E.wait("v", st["um_v"])
            ins = (E.T.matmul(po[:, :], H["btb"][0:64, nt * 128:(nt + 1) * 128],
                              H["um"][:, :], start=False, stop=True)
                   if E.T else None)
        n = E.inc(ins, "p")
        st["grp"][nt] = n
        if nt == NT - 1:
            st["corr_p_done"] = n

    def stats(nt):
        if last_pass:
            return
        if E.T:
            E.wait("v", st["rbcp_v"][nt])
            E.wait("v", st["sq_v"][nt])
            if nt == 0:
                E.wait("c", st["bnd_c_psum_read"])
        ins = (E.T.matmul(P["sx"][:, :], H["onesb"][:, :],
                          H[rbnxt][:, nt * T:(nt + 1) * T],
                          start=(nt == 0), stop=(nt == NT - 1)) if E.T else None)
        sq = H["sqa"] if nt % 2 == 0 else H["sqb_"]
        ins = (E.T.matmul(P["sq"][:, :], H["onesb"][:, :], sq[:, :],
                          start=(nt == 0), stop=(nt == NT - 1)) if E.T else None)
        n = E.inc(ins, "p")
        st["sq_guard"][nt % 2] = n
        if nt == NT - 1:
            st["stats_p"] = n

    def evac(nt):
        po = P["o0"] if nt % 2 == 0 else P["o1"]
        t1 = H["t1a"] if nt % 2 == 0 else H["t1b"]
        ro = H["roa"] if nt % 2 == 0 else H["rob"]
        t2 = H["t2a"] if nt % 2 == 0 else H["t2b"]
        sq = H["sqa"] if nt % 2 == 0 else H["sqb_"]
        if E.C:
            if nt >= 2:
                E.wait("v", st["ro_free_v"][nt - 2])
            else:
                E.wait("v", ro_free_prev.get(NT - 2 + nt, 0))
        rname, coff = rold_src
        rowlen = TOKC if rname == "xT" else T
        src = sap(H[rname], nt * 128 * rowlen + coff, [[rowlen, 128], [1, T]])
        n_ro = E.inc(E.C.dma_start(out=ro[:, :], in_=src) if E.C else None, "dc")
        if E.V:
            E.wait("p", st["grp"][nt])
            if nt == 0:
                E.wait("v", st["ab_v"])
            if gelu and nt >= 2:
                E.wait("c", st["gelu_c"][nt - 2])
            if nt < 2:
                E.wait("c", gelu_c_prev.get(NT - 2 + nt, 0))
        E.vw()
        E.inc(E.V.tensor_tensor(t1[:, :], po[:, :], H["ab"][:, :], ALU.mult)
              if E.V else None, "v")
        st["ev"][nt] = E.c["v"]
        if gelu:
            if E.C:
                E.wait("v", st["ev"][nt])
                if nt >= 2:
                    E.wait("v", st["t2_free_v"][nt - 2])
                else:
                    E.wait("v", t2_free_prev.get(NT - 2 + nt, 0))
            E.inc(E.C.activation(t2[:, :], t1[:, :], AF.Gelu,
                                 bias=bias[:, nt:nt + 1]) if E.C else None, "c")
            g_c = E.c["c"]
            st["gelu_c"][nt] = g_c
            if E.V:
                E.wait("c", g_c)
                E.wait("dc", n_ro)
            E.vw()
            E.inc(E.V.tensor_tensor(ro[:, :], t2[:, :], ro[:, :], ALU.add)
                  if E.V else None, "v")
            st["t2_free_v"][nt] = E.c["v"]
        else:
            if E.V:
                E.wait("dc", n_ro)
            E.vw()
            E.inc(E.V.scalar_tensor_tensor(ro[:, :], t1[:, :], bias[:, nt:nt + 1],
                                           ro[:, :], ALU.add, ALU.add)
                  if E.V else None, "v")
        rnew_v = E.c["v"]
        st["ro_free_v"][nt] = rnew_v
        if E.C:
            E.wait("v", rnew_v)
        dname, dcoff = rnew_dst
        rowlen = TOKC if dname == "outT" else T
        dst = sap(H[dname], nt * 128 * rowlen + dcoff, [[rowlen, 128], [1, T]])
        E.inc(E.C.dma_start(out=dst, in_=ro[:, :]) if E.C else None, "dc")
        if not last_pass:
            E.vw()
            E.inc(E.V.tensor_copy(H[rbnxt][:, nt * T:(nt + 1) * T], ro[:, :])
                  if E.V else None, "v")
            st["rbcp_v"][nt] = E.c["v"]
            if E.V:
                E.wait("p", st["sq_guard"][nt % 2])
            E.vw()
            E.inc(E.V.tensor_tensor(sq[:, :], H[rbnxt][:, nt * T:(nt + 1) * T],
                                    H[rbnxt][:, nt * T:(nt + 1) * T], ALU.mult)
                  if E.V else None, "v")
            st["sq_v"][nt] = E.c["v"]

    if lora:
        mains(0)
        mains(1)
        ffn_prestep_mask(E, st)
        finish(0)
        evac(0)
        finish(1)
        evac(1)
    else:
        mains(0)
        finish(0)
        evac(0)
        mains(1)
        finish(1)
        evac(1)
    for nt in range(2, NT):
        mains(nt)
        finish(nt)
        stats(nt - 2)
        evac(nt)
    stats(NT - 2)
    stats(NT - 1)
    st["mm_done_prev_pass"] = st["wf"][NT - 1]


_NC_CACHE = {}


def _get_nc():
    if "nc" not in _NC_CACHE:
        _NC_CACHE["nc"] = build_nc()
    return _NC_CACHE["nc"]


def kernel(L0_attn_a_x, L0_attn_a_w1, L0_attn_a_w2, L0_attn_a_b2,
           L0_attn_a_sparse_mask, L0_attn_b_mask, L0_attn_b_weight,
           L0_attn_b_bias, L0_ffn_W, L0_ffn_b, L0_ffn_A, L0_ffn_B,
           L0_ffn_gate_w, L0_ffn_gate_b):
    f32 = np.float32
    bf16 = ml_dtypes.bfloat16
    cat = np.ascontiguousarray

    x2 = np.asarray(L0_attn_a_x, f32).reshape(16384, D)
    w1 = np.asarray(L0_attn_a_w1, f32)
    w2t = np.asarray(L0_attn_a_w2, f32).T
    maskS = np.asarray(L0_attn_a_sparse_mask, f32)
    wfT = np.asarray(L0_ffn_W, f32).T
    wbT = np.asarray(L0_attn_b_weight, f32).T
    maskBT = np.asarray(L0_attn_b_mask, np.int32).T
    b2c = cat(np.asarray(L0_attn_a_b2, f32).reshape(NT, 128).T)
    bfc = cat(np.asarray(L0_ffn_b, f32).reshape(NT, 128).T)
    bbc = cat(np.asarray(L0_attn_b_bias, f32).reshape(NT, 128).T)
    ug = np.concatenate([np.asarray(L0_ffn_A, f32).T,
                         np.asarray(L0_ffn_gate_w, f32).T], axis=1)
    ug_img = cat(ug.reshape(FT, 128, 68).transpose(1, 0, 2).reshape(128, FT * 68))
    bt = cat(np.asarray(L0_ffn_B, f32).T)
    gbr = cat(np.asarray(L0_ffn_gate_b, f32).reshape(1, 4))
    iotac = np.arange(64, dtype=f32).reshape(64, 1)
    b32 = cat(np.repeat(np.eye(4, dtype=f32), 32, axis=0).astype(bf16))
    b32t = cat(np.repeat(np.eye(4, dtype=f32), 32, axis=0).T.astype(bf16))
    w1i = cat(w1.reshape(FT, 128, FT, 128).transpose(2, 1, 0, 3)
              .reshape(FT * 128, FT * 128))

    nc = _get_nc()
    in_maps = []
    for c in range(8):
        sl = slice(SL * c, SL * (c + 1))
        wfi = cat(wfT[:, sl].reshape(FT, 128, NTC, 128).transpose(2, 1, 0, 3)
                  .reshape(NTC * 128, FT * 128))
        in_maps.append(dict(
            xT=cat(x2[TOKC * c:TOKC * (c + 1)].T),
            w1i=w1i,
            w2t=cat(w2t[:, sl]),
            maskS=cat(maskS[:, sl]),
            wfi=wfi,
            wbt=cat(wbT[:, sl]),
            maskB=cat(maskBT[:, sl]),
            b2c=b2c, bfc=bfc, bbc=bbc,
            ug=ug_img, bt=bt, gbr=gbr, iotac=iotac, b32=b32, b32t=b32t,
        ))
    res = run_bass_kernel_spmd(nc, in_maps, core_ids=list(range(8)))
    out = np.concatenate([res.results[c]["outT"].T for c in range(8)], axis=0)
    return np.ascontiguousarray(out.reshape(4, D, D))
